# version 15
# baseline (speedup 1.0000x reference)
import numpy as np
import concourse.bass as bass
import concourse.mybir as mybir
from concourse.bass_utils import run_bass_kernel_spmd

F32 = mybir.dt.float32
BF16 = mybir.dt.bfloat16
AF = mybir.ActivationFunctionType
ALU = mybir.AluOpType
AX = mybir.AxisListType
AP = bass.AP

NCORES = 8
BPC = 2
S = 2048
D = 1024
H = 16
CH = 512
NCH_SEQ = S // CH
NTILE = CH // 128
NCH = BPC * NCH_SEQ
EPS = 1e-5
LN8 = float(np.log(0.125))
LIMIT_CH = None


def _plan_program():
    nc = bass.Bass("TRN2", target_bir_lowering=False)

    xin = nc.dram_tensor("xin", [BPC * S, D], F32, kind="ExternalInput")
    wqk_h = nc.dram_tensor("wqk", [16, 8, 128, 128], F32, kind="ExternalInput")
    wv_h = nc.dram_tensor("wv", [2, 8, 128, 512], F32, kind="ExternalInput")
    wo_h = nc.dram_tensor("wo", [8, 128, 1024], F32, kind="ExternalInput")
    win_h = nc.dram_tensor("win", [8, 128, 64], F32, kind="ExternalInput")
    w1_h = nc.dram_tensor("w1", [64, 16], F32, kind="ExternalInput")
    b1_h = nc.dram_tensor("b1", [64, 16], F32, kind="ExternalInput")
    w2_h = nc.dram_tensor("w2", [64, 16], F32, kind="ExternalInput")
    bin_h = nc.dram_tensor("binv", [64, 1], F32, kind="ExternalInput")
    b2_h = nc.dram_tensor("b2v", [64, 1], F32, kind="ExternalInput")
    bout_h = nc.dram_tensor("boutv", [1, 1024], F32, kind="ExternalInput")
    wout_h = nc.dram_tensor("woutm", [64, 1024], F32, kind="ExternalInput")
    ident_h = nc.dram_tensor("ident", [128, 128], F32, kind="ExternalInput")
    dmask_h = nc.dram_tensor("dmask", [128, 896], F32, kind="ExternalInput")

    outx = nc.dram_tensor("outx", [BPC * S, D], F32, kind="ExternalOutput")

    rs_scr = nc.dram_tensor("rs_scr", [16, 512], F32, kind="Internal")
    rr_scr = nc.dram_tensor("rr_scr", [16, 512], F32, kind="Internal")
    scr98 = nc.dram_tensor("scr98", [2, 98, 2], F32, kind="Internal")
    wqk_bf = nc.dram_tensor("wqk_bf", [16, 8, 128, 128], BF16, kind="Internal")
    wv_bf = nc.dram_tensor("wv_bf", [2, 8, 128, 512], BF16, kind="Internal")

    al = nc.alloc_sbuf_tensor
    woT = al("woT", [128, 8, 1024], BF16)
    wq_sb = al("wq_sb", [128, 2, 8, 128], BF16)
    wv_sb = al("wv_sb", [128, 2, 4, 512], BF16)
    win_bf = al("win_bf", [128, 8, 64], BF16)
    wout_bf = al("wout_bf", [64, 1024], BF16)
    w1_sb = al("w1_sb", [64, 16], F32)
    b1_sb = al("b1_sb", [64, 16], F32)
    b1p = al("b1p", [64, 16], F32)
    w2_sb = al("w2_sb", [64, 16], F32)
    bin_sb = al("bin_sb", [64, 1], F32)
    b2bf = al("b2bf", [64, 1], BF16)
    bout_sb = al("bout_sb", [1, 1024], F32)
    bout2 = al("bout2", [1, 1024], F32)
    b1t = al("b1t", [64, 16], F32)
    boute_bf = al("boute_bf", [1, 1024], BF16)
    ident_bf = al("ident_bf", [128, 128], BF16)
    dmask_bf = al("dmask_bf", [128, 896], BF16)
    ones1 = al("ones1", [1, 128], BF16)
    bones = al("bones", [128, 2], BF16)
    eps_t = al("eps_t", [128, 1], F32)
    ln8_t = al("ln8_t", [128, 1], F32)

    x_sb = al("x_sb", [128, 2, 4, 1024], F32)
    xln = al("xln", [128, 2, 1024], BF16)
    xlnT = al("xlnT", [128, 8, 512], BF16)
    QT = al("QT", [128, 2, 8, 512], BF16)
    KT = al("KT", [128, 8, 2048], BF16)
    Vst = al("Vst", [128, 16, 16, 65], BF16)
    Pbuf = al("Pbuf", [128, 2, 2, 512], BF16)
    attnU = al("attnU", [65, 4, 512], F32)
    attnT = al("attnT", [128, 8, 512], BF16)
    attnB = al("attnB", [64, 512], BF16)
    sq_sb = al("sq_sb", [128, 3, 512], BF16)
    rr65 = al("rr65", [65, 2, 512], F32)
    rrb = al("rrb", [65, 2, 512], BF16)
    obc = al("obc", [65, 64], BF16)
    nmax = al("nmax", [128, 2, 4], F32)
    ntmp = al("ntmp", [128, 1], F32)
    nprod = al("nprod", [128, 2], F32)
    negc = al("negc", [128, 2], F32)
    negc2 = al("negc2", [128, 2], F32)
    lnp = al("lnp", [128, 2], F32)
    cbc = al("cbc", [128, 2, 196], F32)
    bn1 = al("bn1", [128, 2, 12], F32)
    mv = al("mv", [128, 2, 2], F32)
    lnv = al("lnv", [128, 2, 1], F32)
    rstd = al("rstd", [128, 2, 1], F32)
    negmr = al("negmr", [128, 2, 1], F32)
    h_sb = al("h_sb", [64, 2, 512], BF16)
    g_sb = al("g_sb", [64, 2, 512], F32)
    gbf = al("gbf", [64, 512], BF16)

    ps = nc.alloc_psum_tensor("ps", [128, 3584], F32)
    ps_tp = nc.alloc_psum_tensor("ps_tp", [128, 1024], BF16)

    DMAQ = {"GP": "gpsimd", "WQ0": "sync", "WQ1": "sync",
            "WV0": "sync", "WV1": "sync",
            "SX0": "sync", "SX1": "sync", "SSB": "sync",
            "SNC": "sync", "SCB": "sync", "SOUT": "sync"}
    QENG = {"T": "tensor", "V": "vector", "S": "scalar", "G": "gpsimd"}
    QENG.update(DMAQ)
    SEM = {k: nc.alloc_semaphore(k) for k in QENG}
    C = {k: 0 for k in SEM}
    plan = {k: [] for k in ("tensor", "vector", "scalar", "gpsimd", "sync")}

    def emit(inc, waits, f):
        w = [(k, v) for (k, v) in waits if v and v > 0]
        if inc in ("V", "S", "G") and C[inc] > 0:
            w.append((inc, C[inc]))

        def run(eng, w=w, inc=inc, f=f):
            for k, v in w:
                eng.wait_ge(SEM[k], v)
            ins = f(eng)
            ins.then_inc(SEM[inc], 16 if inc in DMAQ else 1)
        plan[QENG[inc]].append(run)
        C[inc] += 16 if inc in DMAQ else 1
        return C[inc]

    emit("GP", [], lambda g: g.dma_start(
        woT[:, :, :], AP(wo_h, 0, [[1024, 128], [131072, 8], [1, 1024]])))
    emit("GP", [], lambda g: g.dma_start(
        win_bf[:, :, :], AP(win_h, 0, [[64, 128], [8192, 8], [1, 64]])))
    emit("GP", [], lambda g: g.dma_start(wout_bf[:, :], wout_h[:, :]))
    emit("GP", [], lambda g: g.dma_start(w1_sb[:, :], w1_h[:, :]))
    emit("GP", [], lambda g: g.dma_start(b1_sb[:, :], b1_h[:, :]))
    emit("GP", [], lambda g: g.dma_start(w2_sb[:, :], w2_h[:, :]))
    emit("GP", [], lambda g: g.dma_start(bin_sb[:, :], bin_h[:, :]))
    emit("GP", [], lambda g: g.dma_start(b2bf[:, :], b2_h[:, :]))
    emit("GP", [], lambda g: g.dma_start(bout_sb[:, :], bout_h[:, :]))
    emit("GP", [], lambda g: g.dma_start(ident_bf[:, :], ident_h[:, :]))
    emit("GP", [], lambda g: g.dma_start(dmask_bf[:, :], dmask_h[:, :]))
    emit("GP", [], lambda g: g.dma_start(wqk_bf[:, :, :, :], wqk_h[:, :, :, :]))
    gp_ld = emit("GP", [], lambda g: g.dma_start(wv_bf[:, :, :, :], wv_h[:, :, :, :]))
    emit("G", [], lambda g: g.memset(ones1[:, :], 1.0))
    emit("G", [], lambda g: g.memset(
        AP(Vst, 64, [[16640, 128], [1040, 16], [65, 16], [1, 1]]), 1.0))
    emit("G", [], lambda g: g.memset(eps_t[:, :], EPS))
    emit("G", [], lambda g: g.memset(ln8_t[:, :], LN8))
    emit("G", [], lambda g: g.memset(Pbuf[:, :, :, :], 0.0))
    emit("G", [], lambda g: g.memset(obc[:, :], 1.0))
    emit("G", [], lambda g: g.memset(bones[:, :], 0.0))
    emit("G", [], lambda g: g.memset(bones[0:64, 0:1], 1.0))
    g_pro = emit("G", [], lambda g: g.memset(bones[64:128, 1:2], 1.0))

    def x_load(ch):
        off = ch * CH * D
        slot = ch % 2

        def f(sy, off=off, slot=slot):
            return sy.dma_start(x_sb[:, slot, :, :],
                                AP(xin, off, [[1024, 128], [131072, 4], [1, 1024]]))
        return f
    xld = {0: emit("SX0", [], x_load(0)), 1: emit("SX1", [], x_load(1))}

    t_be = emit("T", [("GP", gp_ld)], lambda t: (
        t.matmul(ps[0:1, 0:512], b2bf[:, :], wout_bf[:, 0:512], start=True, stop=True),
        t.matmul(ps[0:1, 512:1024], b2bf[:, :], wout_bf[:, 512:1024],
                 start=True, stop=True))[-1])
    emit("V", [("T", t_be)],
         lambda v: v.tensor_add(bout2[:, :], bout_sb[:, :], ps[0:1, 0:1024]))
    emit("V", [], lambda v: v.tensor_copy(boute_bf[:, :], bout2[:, :]))
    emit("V", [("GP", gp_ld)],
         lambda v: v.tensor_scalar(b1t[:, :], w1_sb[:, :], bin_sb[:, 0:1], None, ALU.mult))
    v_pro = emit("V", [], lambda v: v.tensor_add(b1p[:, :], b1t[:, :], b1_sb[:, :]))
    emit("V", [], lambda v: v.memset(ps[0:128, 3072:3584], 0.0))
    v_psz = emit("V", [], lambda v: v.memset(ps[0:128, 0:1024], 0.0))

    M = dict(last_tp_evac=0, outproj_T=0, last_hS=0,
             last_F7_V=v_psz, last_exp_S=0, xln_T=0,
             qn_last_V=0, f1_T=0, bcmm_T0=0, rr_sub_V=0)
    mvS = [0, 0]
    xln_slot_T = [0, 0]
    b_slot_AV_T = [0, 0]
    bc_slot_V = [0, 0]
    hb_slot_V = [0, 0]
    stageB_SY = [0]
    wq_slot_T = [0, 0]
    wv_slot_T = [0, 0]
    attnU_mult_V = [0, 0, 0, 0]
    attn_bank_evac_V = [0, 0]
    evac_qk = {}
    v_evac = {}
    b45 = [v_psz, v_psz]
    mult_cnt = {}
    a_done = {}
    b_doneV = {}
    b_doneT = {}
    bc_of = {}
    gnm_of = {}

    def emit_A_chain(ch, ti):
        slot = ch % 2
        ms = ti % 2
        xv = x_sb[:, slot, ti, :]
        fw = [("SX%d" % (ch % 2), xld[ch])] if ti == 0 else []
        emit("V", fw + [("S", mvS[ms])],
             lambda v, xv=xv, ms=ms: v.bn_stats(bn1[:, ms, 0:6], xv[:, 0:512]))
        emit("V", [],
             lambda v, xv=xv, ms=ms: v.bn_stats(bn1[:, ms, 6:12], xv[:, 512:1024]))
        vc = emit("V", [], lambda v, ms=ms: v.bn_aggr(
            mv[:, ms, :], bn1[:, ms, :].rearrange("p (g s) -> p g s", s=6)))
        emit("S", [("V", vc), ("G", g_pro)], lambda s, ms=ms: s.activation(
            lnv[:, ms, :], mv[:, ms, 1:2], AF.Ln, bias=eps_t[:, 0:1], scale=1.0))
        mvS[ms] = emit("S", [], lambda s, ms=ms: s.activation(
            rstd[:, ms, :], lnv[:, ms, :], AF.Exp, bias=0.0, scale=-0.5))
        emit("V", [("S", mvS[ms])], lambda v, ms=ms: v.tensor_scalar(
            negmr[:, ms, :], mv[:, ms, 0:1], rstd[:, ms, 0:1], -1.0,
            ALU.mult, ALU.mult))
        vx = emit("V", [("T", xln_slot_T[ms])],
                  lambda v, xv=xv, ms=ms: v.tensor_scalar(
                      xln[:, ms, :], xv, rstd[:, ms, 0:1], negmr[:, ms, 0:1],
                      ALU.mult, ALU.add))

        def tp_part(ms=ms, ti=ti, vx=vx):
            def tf(t):
                ins = None
                for c in range(8):
                    ins = t.transpose(ps_tp[:, c * 128:(c + 1) * 128],
                                      xln[:, ms, c * 128:(c + 1) * 128],
                                      ident_bf[:, :])
                return ins
            xln_slot_T[ms] = emit("T", [("V", vx), ("V", M["last_tp_evac"]),
                                        ("GP", gp_ld)], tf)
            M["last_tp_evac"] = emit("V", [("T", xln_slot_T[ms])],
                                     lambda v, ti=ti: v.tensor_copy(
                                         AP(xlnT, ti * 128,
                                            [[4096, 128], [512, 8], [1, 128]]),
                                         ps_tp[:, 0:1024]))
        return tp_part

    def gen_AB(ch):
        gs = ch % NCH_SEQ
        qsl = ch % 2
        if gs == 0:
            gnm_of[ch] = emit("G", [("V", C["V"])],
                              lambda g: g.memset(nmax[:, :, :], 0.0))
        else:
            gnm_of[ch] = gnm_of[ch - 1]
        g_nm = gnm_of[ch]
        tp_pend = []
        for ti in range(NTILE):
            tp_pend.append(emit_A_chain(ch, ti))
            if ti >= 1:
                tp_pend.pop(0)()
            yield
        while tp_pend:
            tp_pend.pop(0)()
            yield
        a_done = M["last_tp_evac"]

        wq_gd = {}
        wq_gd[0] = emit("WQ0", [("T", wq_slot_T[0]), ("GP", gp_ld)],
                        lambda g: g.dma_start(
                            wq_sb[:, 0, :, :],
                            AP(wqk_bf, 0, [[128, 128], [16384, 8], [1, 128]])))
        qn_prev = 0
        pend_norm = []
        sqT = [0, 0, 0]

        def flush_norm():
            nonlocal qn_prev
            og, gsq = pend_norm.pop(0)
            hp = og % 8
            tpj = 32 * (hp % 4)
            sq4 = og % 3

            def nf(t, tpj=tpj, sq4=sq4):
                return t.matmul(ps[tpj:tpj + 2, 3072:3584],
                                bones[:, :], sq_sb[:, sq4, :], start=True, stop=True,
                                tile_position=(0, tpj))
            nrm_T = emit("T", [("G", gsq), ("V", v_psz), ("V", qn_prev),
                               ("S", M["last_hS"])], nf)
            sqT[sq4] = nrm_T
            if hp % 4 == 3:
                emit("V", [("T", nrm_T)],
                     lambda v: v.reduce_max(ntmp[:, :],
                                            ps[0:128, 3072:3584], AX.X))
                cidx = og // 4

                def mxf(v, cidx=cidx, gs=gs):
                    return v.tensor_max(nmax[:, gs % 2, cidx:cidx + 1],
                                        nmax[:, (gs + 1) % 2, cidx:cidx + 1],
                                        ntmp[:, :])
                qn_prev = emit("V", [("G", g_nm)], mxf)

        for og in range(16):
            ob = og % 2
            if og < 15:
                off = (og + 1) * 8 * 128 * 128
                ob2 = (og + 1) % 2

                def wf(g, off=off, ob2=ob2):
                    return g.dma_start(
                        wq_sb[:, ob2, :, :],
                        AP(wqk_bf, off, [[128, 128], [16384, 8], [1, 128]]))
                wq_gd[og + 1] = emit("WQ%d" % ob2, [("T", wq_slot_T[ob2])], wf)

            def qf(t, ob=ob):
                ins = None
                for c in range(8):
                    ins = t.matmul(ps[:, (4 + ob) * 512:(5 + ob) * 512],
                                   wq_sb[:, ob, c, :], xlnT[:, c, :],
                                   start=(c == 0), stop=(c == 7))
                return ins
            wq_slot_T[ob] = emit("T", [("WQ%d" % ob, wq_gd[og]), ("V", a_done),
                                       ("V", b45[ob])], qf)
            hp = og % 8
            if og < 8:
                def ef(v, hp=hp, ob=ob, qsl=qsl):
                    return v.tensor_copy(QT[:, qsl, hp, :],
                                         ps[:, (4 + ob) * 512:(5 + ob) * 512])
                src = QT[:, qsl, hp, :]
            else:
                def ef(v, hp=hp, ob=ob, gs=gs):
                    return v.tensor_copy(KT[:, hp, gs * 512:(gs + 1) * 512],
                                         ps[:, (4 + ob) * 512:(5 + ob) * 512])
                src = KT[:, hp, gs * 512:(gs + 1) * 512]
            evac_qk[og] = emit("V", [("T", wq_slot_T[ob])], ef)
            b45[ob] = evac_qk[og]
            gsq = emit("G", [("V", evac_qk[og]), ("T", sqT[og % 3])],
                       lambda g, src=src, sq4=og % 3: g.tensor_mul(
                           sq_sb[:, sq4, :], src, src))
            pend_norm.append((og, gsq))
            if len(pend_norm) > 2:
                flush_norm()
            yield
        while pend_norm:
            flush_norm()
        M["qn_last_V"] = qn_prev

        for vg in range(2):
            for tp in range(2):
                for hf in range(2):
                    off = (vg * 8 + hf * 4) * 128 * 512

                    def vf(g, off=off, hf=hf):
                        return g.dma_start(
                            wv_sb[:, hf, :, :],
                            AP(wv_bf, off, [[512, 128], [65536, 4], [1, 512]]))
                    wv_gd = emit("WV%d" % hf, [("T", wv_slot_T[hf]), ("GP", gp_ld)], vf)
                    for cc in range(4):
                        for ti in (2 * tp, 2 * tp + 1):
                            def vmf(t, hf=hf, cc=cc, ti=ti):
                                return t.matmul(
                                    ps[:, (4 + ti % 2) * 512:(5 + ti % 2) * 512],
                                    xlnT[:, hf * 4 + cc, ti * 128:(ti + 1) * 128],
                                    wv_sb[:, hf, cc, :],
                                    start=(hf == 0 and cc == 0),
                                    stop=(hf == 1 and cc == 3))
                            w = [("WV%d" % hf, wv_gd)]
                            if hf == 0 and cc == 0:
                                w += [("V", b45[0]), ("V", b45[1])]
                            wv_slot_T[hf] = emit("T", w, vmf)
                for ti in (2 * tp, 2 * tp + 1):
                    def vef(v, vg=vg, ti=ti, gs=gs):
                        j = gs * 4 + ti
                        return v.tensor_copy(
                            AP(Vst, (vg * 8) * 1040 + j * 65,
                               [[16640, 128], [1040, 8], [1, 64]]),
                            ps[:, (4 + ti % 2) * 512:(5 + ti % 2) * 512])
                    v_evac[(vg, tp, ti)] = emit("V", [("T", wv_slot_T[1])], vef)
                    b45[ti % 2] = v_evac[(vg, tp, ti)]
                yield
        b_doneV[ch] = v_evac[(1, 1, 3)]
        b_doneT[ch] = wv_slot_T[1]

        vp = emit("V", [("V", M["qn_last_V"])],
                  lambda v, gs=gs: v.tensor_mul(
                      nprod[:, :], nmax[:, gs % 2, 0:2], nmax[:, gs % 2, 2:4]))
        emit("S", [("V", vp), ("G", g_pro)], lambda s: s.activation(
            lnp[:, :], nprod[:, :], AF.Ln, bias=eps_t[:, 0:1], scale=1.0))
        sp = emit("S", [("G", g_pro)], lambda s: s.activation(
            negc[:, :], lnp[:, :], AF.Exp, bias=ln8_t[0:128, 0:1], scale=0.5))
        vn = emit("V", [("S", sp)], lambda v: v.tensor_scalar(
            negc2[:, :], negc[:, :], -1.0, None, ALU.mult))
        emit("SNC", [("V", vn), ("SCB", C["SCB"])],
             lambda sy, qsl=qsl: sy.dma_start(scr98[qsl, :, :], negc2[0:98, :]))
        bc_of[ch] = emit("SCB", [("SNC", C["SNC"]), ("S", M["last_exp_S"])],
                         lambda sy, qsl=qsl: sy.dma_start(
                             cbc[:, qsl, :],
                             AP(scr98, qsl * 196, [[0, 128], [1, 196]])))
        yield

    def emit_C(ch, gens):
        gs = ch % NCH_SEQ
        qsl = ch % 2

        def pull():
            while gens:
                try:
                    next(gens[0])
                    return
                except StopIteration:
                    gens.pop(0)

        def drain_first():
            if gens:
                for _ in gens[0]:
                    pass
                gens.pop(0)
        pending_post = []
        for h in range(H):
            hp, ab = h // 2, h % 2
            pb = 64 * ab
            abk = 2 + (h % 2)
            nb2 = 2 * (gs + 1)
            for b2 in range(nb2):
                if b2 == 1 and pending_post:
                    pending_post.pop(0)()
                diag = (b2 >= 2 * gs)
                bs2 = b2 % 2

                def sf(t, b2=b2, hp=hp, pb=pb, diag=diag, gs=gs, qsl=qsl):
                    ins = None
                    for jj2 in range(2):
                        j = 2 * b2 + jj2
                        jg = j - 4 * gs
                        q0 = jg * 128
                        ins = t.matmul(ps[:, jj2 * 512:(jj2 + 1) * 512],
                                       KT[pb:pb + 64, hp, j * 128:(j + 1) * 128],
                                       QT[pb:pb + 64, qsl, hp, :],
                                       start=True, stop=diag is False)
                        if diag:
                            ins = t.matmul(
                                ps[:, jj2 * 512:jj2 * 512 + q0 + 128],
                                ident_bf[:, :],
                                dmask_bf[:, (3 - jg) * 128:(4 - jg) * 128 + q0],
                                start=False, stop=True)
                    return ins
                sw = [("S", M["last_exp_S"]), ("V", b_doneV[ch]), ("G", g_pro)]
                sc_T = emit("T", sw, sf)

                ui = 64 * ((h // 2) % 4) + 2 * (h % 2) + (h // 8)

                def xf(s, bs2=bs2, ui=ui, qsl=qsl):
                    return s.activation(
                        Pbuf[:, bs2, :, :].rearrange("p a b -> p (a b)"),
                        ps[:, 0:1024], AF.Exp, bias=cbc[:, qsl, ui:ui + 1],
                        scale=0.125)
                M["last_exp_S"] = emit("S", [("T", sc_T), ("SCB", bc_of[ch]),
                                             ("T", b_slot_AV_T[bs2])], xf)
                pull()
                pull()

                def af(t, b2=b2, bs2=bs2, h=h, abk=abk, nb2=nb2):
                    ins = None
                    for jj2 in range(2):
                        j = 2 * b2 + jj2
                        ins = t.matmul(ps[0:65, abk * 512:(abk + 1) * 512],
                                       Vst[:, h, j, 0:65],
                                       Pbuf[:, bs2, jj2, :],
                                       start=(b2 == 0 and jj2 == 0),
                                       stop=(b2 == nb2 - 1 and jj2 == 1))
                    return ins
                avw = [("S", M["last_exp_S"])]
                if b2 == 0:
                    avw.append(("V", attn_bank_evac_V[h % 2]))
                b_slot_AV_T[bs2] = emit("T", avw, af)
            av_T = b_slot_AV_T[(nb2 - 1) % 2]
            us = h % 4

            def uf(v, us=us, abk=abk):
                return v.tensor_copy(attnU[:, us, :],
                                     ps[0:65, abk * 512:(abk + 1) * 512])
            ue = emit("V", [("T", av_T), ("V", attnU_mult_V[us])], uf)
            if h == 1 and gens and getattr(gens[0], "is_de", False):
                drain_first()
            emit("S", [("V", ue)],
                 lambda s, us=us: s.activation(
                     rr65[64:65, 0, :], attnU[64:65, us, :], AF.Ln,
                     bias=0.0, scale=1.0))
            sr = emit("S", [("V", M["rr_sub_V"])], lambda s: s.activation(
                rr65[64:65, 1, :], rr65[64:65, 0, :], AF.Exp,
                bias=0.0, scale=-1.0))
            emit("V", [("T", M["bcmm_T0"]), ("S", sr)],
                 lambda v: v.tensor_copy(rrb[64:65, 0, :], rr65[64:65, 1, :]))
            s2 = emit("V", [], lambda v: v.tensor_sub(
                rrb[64:65, 1, :], rr65[64:65, 1, :], rrb[64:65, 0, :]))
            M["rr_sub_V"] = s2

            def post(h=h, hp=hp, ab=ab, us=us, abk=abk, s2=s2, ch=ch):
                def bcm(t, abk=abk):
                    t.matmul(ps[0:64, abk * 512:(abk + 1) * 512],
                             obc[64:65, 0:64], rrb[64:65, 0, :],
                             start=True, stop=False)
                    return t.matmul(ps[0:64, abk * 512:(abk + 1) * 512],
                                    obc[64:65, 0:64], rrb[64:65, 1, :],
                                    start=False, stop=True)
                bcmm = emit("T", [("V", s2)], bcm)
                M["bcmm_T0"] = bcmm
                if ab == 0:
                    dst = attnT[0:64, hp, :]
                else:
                    dst = attnB[:, :]

                def mf(v, dst=dst, us=us, abk=abk):
                    return v.tensor_mul(dst, attnU[0:64, us, :],
                                        ps[0:64, abk * 512:(abk + 1) * 512])
                mw = [("T", bcmm), ("T", M["outproj_T"])]
                if ab == 1:
                    mw.append(("SSB", stageB_SY[0]))
                mult_cnt[(ch, h)] = emit("V", mw, mf)
                attnU_mult_V[us] = mult_cnt[(ch, h)]
                attn_bank_evac_V[h % 2] = mult_cnt[(ch, h)]
                if ab == 1:
                    def sbf(sy, hp=hp):
                        return sy.dma_start(attnT[64:128, hp, :], attnB[:, :])
                    stageB_SY[0] = emit("SSB", [("V", mult_cnt[(ch, h)])], sbf)
            pending_post.append(post)
        while pending_post:
            pending_post.pop(0)()

    def _gen_DE(ch):
        slot = ch % 2
        attn_done_V = mult_cnt[(ch, 14)]
        attn_done_SY = stageB_SY[0]
        d3_V = 0
        tp_pend = []
        for ti in range(NTILE):
            def of(t, ti=ti):
                ins = None
                for hp in range(8):
                    for mg in range(2):
                        ins = t.matmul(ps[:, (4 + mg) * 512:(5 + mg) * 512],
                                       attnT[:, hp, ti * 128:(ti + 1) * 128],
                                       woT[:, hp, mg * 512:(mg + 1) * 512],
                                       start=(hp == 0), stop=(hp == 7))
                return ins
            ow = [("V", attn_done_V), ("SSB", attn_done_SY), ("V", d3_V),
                  ("V", b45[0]), ("V", b45[1])]
            M["outproj_T"] = emit("T", ow, of)
            op_T = M["outproj_T"]

            def df(v, ti=ti, slot=slot):
                return v.tensor_add(x_sb[:, slot, ti, :], x_sb[:, slot, ti, :],
                                    ps[:, 2048:3072])
            d3_V = emit("V", [("T", op_T)], df)
            b45[0] = d3_V
            b45[1] = d3_V
            tp_pend.append(emit_A_chain(ch, ti))
            if ti >= 1:
                tp_pend.pop(0)()
            yield

        while tp_pend:
            tp_pend.pop(0)()
            yield

        def f1(t):
            ins = None
            for c in range(8):
                ins = t.matmul(ps[0:64, 3072:3584], win_bf[:, c, :], xlnT[:, c, :],
                               start=(c == 0), stop=(c == 7))
            return ins
        M["f1_T"] = emit("T", [("V", M["last_tp_evac"]), ("S", M["last_hS"]),
                               ("V", M["qn_last_V"])], f1)
        yield

    def _gen_F(ch):
        slot = ch % 2
        f1_T = M["f1_T"]
        for i in range(16):
            hs = i % 2

            def hf2(s, i=i, hs=hs):
                return s.activation(h_sb[:, hs, :], ps[0:64, 3072:3584],
                                    AF.Relu, bias=b1p[:, i:i + 1],
                                    scale=w1_sb[:, i:i + 1])
            sH = emit("S", [("T", f1_T), ("V", hb_slot_V[hs])], hf2)
            M["last_hS"] = sH
            if i == 0:
                mac_V = emit("V", [("S", sH)],
                             lambda v: v.tensor_scalar(g_sb[:, 0, :], h_sb[:, 0, :],
                                                       w2_sb[:, 0:1], None, ALU.mult))
            else:
                def mi(v, i=i, hs=hs):
                    return v.scalar_tensor_tensor(g_sb[:, i % 2, :], h_sb[:, hs, :],
                                                  w2_sb[:, i:i + 1],
                                                  g_sb[:, (i + 1) % 2, :],
                                                  ALU.mult, ALU.add)
                mac_V = emit("V", [("S", sH)], mi)
            hb_slot_V[hs] = mac_V
            if i % 4 == 3:
                yield
        gb_V = emit("V", [], lambda v: v.tensor_copy(gbf[:, :], g_sb[:, 1, :]))
        yield
        f7_V = 0
        for ti in range(NTILE):
            def f6(t, ti=ti):
                t.matmul(ps[:, 2048:2560], gbf[:, ti * 128:(ti + 1) * 128],
                         wout_bf[:, 0:512], start=True, stop=False)
                t.matmul(ps[:, 2560:3072], gbf[:, ti * 128:(ti + 1) * 128],
                         wout_bf[:, 512:1024], start=True, stop=False)
                t.matmul(ps[:, 2048:2560], ones1[:, :], boute_bf[:, 0:512],
                         start=False, stop=True)
                return t.matmul(ps[:, 2560:3072], ones1[:, :], boute_bf[:, 512:1024],
                                start=False, stop=True)
            f6_T = emit("T", [("V", gb_V), ("V", f7_V if ti > 0 else b45[0]),
                              ("V", b45[1]), ("G", g_pro)], f6)

            def f7(v, ti=ti, slot=slot):
                return v.tensor_add(x_sb[:, slot, ti, :], x_sb[:, slot, ti, :],
                                    ps[:, 2048:3072])
            f7_V = emit("V", [("T", f6_T)], f7)
            b45[0] = f7_V
            b45[1] = f7_V
            yield
        M["last_F7_V"] = f7_V

        off = ch * CH * D

        def f8(sy, off=off, slot=slot):
            return sy.dma_start(AP(outx, off, [[1024, 128], [131072, 4], [1, 1024]]),
                                x_sb[:, slot, :, :])
        emit("SOUT", [("V", f7_V)], f8)
        if ch + 2 < NCH:
            xld[ch + 2] = emit("SX%d" % (ch % 2), [("SOUT", C["SOUT"])],
                               x_load(ch + 2))
        yield

    class _DeGen:
        is_de = True

        def __init__(self, g):
            self.g = g

        def __iter__(self):
            return self.g

        def __next__(self):
            return next(self.g)

    def gen_DE(ch):
        return _DeGen(_gen_DE(ch))

    nch = NCH if LIMIT_CH is None else LIMIT_CH
    for _ in gen_AB(0):
        pass
    pend = []
    for ch in range(nch):
        boundary = (ch + 1 < nch) and ((ch + 1) % NCH_SEQ == 0)
        gens = list(pend)
        if (ch + 1 < nch) and not boundary:
            gens.append(gen_AB(ch + 1))
        emit_C(ch, gens)
        for g in gens:
            for _ in g:
                pass
        if boundary:
            for _ in gen_DE(ch):
                pass
            for _ in _gen_F(ch):
                pass
            for _ in gen_AB(ch + 1):
                pass
            pend = []
        else:
            pend = [gen_DE(ch), _gen_F(ch)]
    for g in pend:
        for _ in g:
            pass

    final_sy = C["SOUT"]

    def fin(eng):
        eng.wait_ge(SEM["SOUT"], final_sy)
        eng.nop()
    plan["sync"].append(fin)

    with nc.Block() as block:
        @block.tensor
        def _(t):
            for f in plan["tensor"]:
                f(t)

        @block.vector
        def _(v):
            for f in plan["vector"]:
                f(v)

        @block.scalar
        def _(s):
            for f in plan["scalar"]:
                f(s)

        @block.gpsimd
        def _(g):
            for f in plan["gpsimd"]:
                f(g)

        @block.sync
        def _(sy):
            for f in plan["sync"]:
                f(sy)

    return nc


def _host_layouts(inputs):
    w_qkv = np.asarray(inputs["L0_attn_w_qkv"], dtype=np.float32)
    w_o = np.asarray(inputs["L0_attn_w_o"], dtype=np.float32)
    Win = np.asarray(inputs["L0_ffn_Win"], dtype=np.float32)
    bin_ = np.asarray(inputs["L0_ffn_bin_"], dtype=np.float32)
    W1 = np.asarray(inputs["L0_ffn_W1"], dtype=np.float32)
    b1 = np.asarray(inputs["L0_ffn_b1"], dtype=np.float32)
    W2 = np.asarray(inputs["L0_ffn_W2"], dtype=np.float32)
    b2 = np.asarray(inputs["L0_ffn_b2"], dtype=np.float32)
    Wout = np.asarray(inputs["L0_ffn_Wout"], dtype=np.float32)
    bout = np.asarray(inputs["L0_ffn_bout"], dtype=np.float32)

    wq = w_qkv[0].reshape(8, 128, 1024)
    wk = w_qkv[1].reshape(8, 128, 1024)
    wqk = np.concatenate([wq, wk], axis=0)
    wqk = wqk.transpose(0, 2, 1).reshape(16, 8, 128, 128)
    wvm = w_qkv[2].reshape(1024, 1024).T
    wv = wvm.reshape(8, 128, 2, 512).transpose(2, 0, 1, 3)
    wo = w_o.T.reshape(8, 128, 1024)
    win = Win.reshape(8, 128, 64)

    ident = np.eye(128, dtype=np.float32)
    p = np.arange(128)[:, None]
    w = np.arange(896)[None, :]
    dmask = np.where(p > w - 384, np.float32(-1e30), np.float32(0.0))

    return {
        "wqk": np.ascontiguousarray(wqk),
        "wv": np.ascontiguousarray(wv),
        "wo": np.ascontiguousarray(wo),
        "win": np.ascontiguousarray(win),
        "w1": np.ascontiguousarray(W1),
        "b1": np.ascontiguousarray(b1),
        "w2": np.ascontiguousarray(W2),
        "binv": np.ascontiguousarray(bin_.reshape(64, 1)),
        "b2v": np.ascontiguousarray(b2.reshape(64, 1)),
        "boutv": np.ascontiguousarray(bout.reshape(1, 1024)),
        "woutm": np.ascontiguousarray(Wout),
        "ident": ident,
        "dmask": np.ascontiguousarray(dmask),
    }


def kernel(**inputs):
    x = np.asarray(inputs["L0_attn_x"], dtype=np.float32)
    shared = _host_layouts(inputs)
    nc = _plan_program()
    in_maps = []
    for c in range(NCORES):
        m = dict(shared)
        m["xin"] = np.ascontiguousarray(x[c * BPC:(c + 1) * BPC].reshape(BPC * S, D))
        in_maps.append(m)
    res = run_bass_kernel_spmd(nc, in_maps, core_ids=list(range(NCORES)))
    out = np.concatenate(
        [np.asarray(r["outx"]).reshape(BPC, S, D) for r in res.results], axis=0)
    return out


# version 16
# speedup vs baseline: 1.0999x; 1.0999x over previous
import numpy as np
import concourse.bass as bass
import concourse.mybir as mybir
from concourse.bass_utils import run_bass_kernel_spmd

F32 = mybir.dt.float32
BF16 = mybir.dt.bfloat16
AF = mybir.ActivationFunctionType
ALU = mybir.AluOpType
AX = mybir.AxisListType
AP = bass.AP

NCORES = 8
BPC = 2
S = 2048
D = 1024
H = 16
CH = 512
NCH_SEQ = S // CH
NTILE = CH // 128
NCH = BPC * NCH_SEQ
EPS = 1e-5
LN8 = float(np.log(0.125))
LIMIT_CH = None


def _plan_program():
    nc = bass.Bass("TRN2", target_bir_lowering=False)

    xin = nc.dram_tensor("xin", [BPC * S, D], F32, kind="ExternalInput")
    wqk_h = nc.dram_tensor("wqk", [16, 8, 128, 128], F32, kind="ExternalInput")
    wv_h = nc.dram_tensor("wv", [2, 8, 128, 512], F32, kind="ExternalInput")
    wo_h = nc.dram_tensor("wo", [8, 128, 1024], F32, kind="ExternalInput")
    win_h = nc.dram_tensor("win", [8, 128, 64], F32, kind="ExternalInput")
    w1_h = nc.dram_tensor("w1", [64, 16], F32, kind="ExternalInput")
    b1_h = nc.dram_tensor("b1", [64, 16], F32, kind="ExternalInput")
    w2_h = nc.dram_tensor("w2", [64, 16], F32, kind="ExternalInput")
    bin_h = nc.dram_tensor("binv", [64, 1], F32, kind="ExternalInput")
    b2_h = nc.dram_tensor("b2v", [64, 1], F32, kind="ExternalInput")
    bout_h = nc.dram_tensor("boutv", [1, 1024], F32, kind="ExternalInput")
    wout_h = nc.dram_tensor("woutm", [64, 1024], F32, kind="ExternalInput")
    ident_h = nc.dram_tensor("ident", [128, 128], F32, kind="ExternalInput")
    dmask_h = nc.dram_tensor("dmask", [128, 896], F32, kind="ExternalInput")

    outx = nc.dram_tensor("outx", [BPC * S, D], F32, kind="ExternalOutput")

    rs_scr = nc.dram_tensor("rs_scr", [16, 512], F32, kind="Internal")
    rr_scr = nc.dram_tensor("rr_scr", [16, 512], F32, kind="Internal")
    scr98 = nc.dram_tensor("scr98", [2, 98, 2], F32, kind="Internal")
    wqk_bf = nc.dram_tensor("wqk_bf", [16, 8, 128, 128], BF16, kind="Internal")
    wv_bf = nc.dram_tensor("wv_bf", [2, 8, 128, 512], BF16, kind="Internal")

    al = nc.alloc_sbuf_tensor
    woT = al("woT", [128, 8, 1024], BF16)
    wq_sb = al("wq_sb", [128, 2, 8, 128], BF16)
    wv_sb = al("wv_sb", [128, 2, 4, 512], BF16)
    win_bf = al("win_bf", [128, 8, 64], BF16)
    wout_bf = al("wout_bf", [64, 1024], BF16)
    w1_sb = al("w1_sb", [64, 16], F32)
    b1_sb = al("b1_sb", [64, 16], F32)
    b1p = al("b1p", [64, 16], F32)
    w2_sb = al("w2_sb", [64, 16], F32)
    bin_sb = al("bin_sb", [64, 1], F32)
    b2bf = al("b2bf", [64, 1], BF16)
    bout_sb = al("bout_sb", [1, 1024], F32)
    bout2 = al("bout2", [1, 1024], F32)
    b1t = al("b1t", [64, 16], F32)
    boute_bf = al("boute_bf", [1, 1024], BF16)
    ident_bf = al("ident_bf", [128, 128], BF16)
    dmask_bf = al("dmask_bf", [128, 896], BF16)
    ones1 = al("ones1", [1, 128], BF16)
    bones = al("bones", [128, 2], BF16)
    eps_t = al("eps_t", [128, 1], F32)
    ln8_t = al("ln8_t", [128, 1], F32)

    x_sb = al("x_sb", [128, 2, 4, 1024], F32)
    xln = al("xln", [128, 2, 1024], BF16)
    xlnT = al("xlnT", [128, 8, 512], BF16)
    QT = al("QT", [128, 2, 8, 512], BF16)
    KT = al("KT", [128, 8, 2048], BF16)
    Vst = al("Vst", [128, 16, 16, 65], BF16)
    Pbuf = al("Pbuf", [128, 2, 2, 512], BF16)
    attnU = al("attnU", [65, 4, 512], F32)
    attnT = al("attnT", [128, 8, 512], BF16)
    attnB = al("attnB", [64, 512], BF16)
    sq_sb = al("sq_sb", [128, 3, 512], BF16)
    rr65 = al("rr65", [65, 2, 512], F32)
    rrb = al("rrb", [65, 2, 512], BF16)
    obc = al("obc", [65, 64], BF16)
    nmax = al("nmax", [128, 2, 4], F32)
    ntmp = al("ntmp", [128, 1], F32)
    nprod = al("nprod", [128, 2], F32)
    negc = al("negc", [128, 2], F32)
    negc2 = al("negc2", [128, 2], F32)
    lnp = al("lnp", [128, 2], F32)
    cbc = al("cbc", [128, 2, 196], F32)
    bn1 = al("bn1", [128, 2, 12], F32)
    mv = al("mv", [128, 2, 2], F32)
    lnv = al("lnv", [128, 2, 1], F32)
    rstd = al("rstd", [128, 2, 1], F32)
    negmr = al("negmr", [128, 2, 1], F32)
    h_sb = al("h_sb", [64, 2, 512], BF16)
    g_sb = al("g_sb", [64, 2, 512], F32)
    gbf = al("gbf", [64, 512], BF16)

    ps = nc.alloc_psum_tensor("ps", [128, 3584], F32)
    ps_tp = nc.alloc_psum_tensor("ps_tp", [128, 1024], BF16)

    DMAQ = {"GP": "gpsimd", "WQ0": "sync", "WQ1": "sync",
            "WV0": "sync", "WV1": "sync",
            "SX0": "sync", "SX1": "sync", "SSB": "sync",
            "SNC": "sync", "SCB": "sync", "SOUT": "sync"}
    QENG = {"T": "tensor", "V": "vector", "S": "scalar", "G": "gpsimd"}
    QENG.update(DMAQ)
    SEM = {k: nc.alloc_semaphore(k) for k in QENG}
    C = {k: 0 for k in SEM}
    plan = {k: [] for k in ("tensor", "vector", "scalar", "gpsimd", "sync")}

    def emit(inc, waits, f):
        w = [(k, v) for (k, v) in waits if v and v > 0]
        if inc in ("V", "S", "G") and C[inc] > 0:
            w.append((inc, C[inc]))

        def run(eng, w=w, inc=inc, f=f):
            for k, v in w:
                eng.wait_ge(SEM[k], v)
            ins = f(eng)
            ins.then_inc(SEM[inc], 16 if inc in DMAQ else 1)
        plan[QENG[inc]].append(run)
        C[inc] += 16 if inc in DMAQ else 1
        return C[inc]

    emit("GP", [], lambda g: g.dma_start(
        woT[:, :, :], AP(wo_h, 0, [[1024, 128], [131072, 8], [1, 1024]])))
    emit("GP", [], lambda g: g.dma_start(
        win_bf[:, :, :], AP(win_h, 0, [[64, 128], [8192, 8], [1, 64]])))
    emit("GP", [], lambda g: g.dma_start(wout_bf[:, :], wout_h[:, :]))
    emit("GP", [], lambda g: g.dma_start(w1_sb[:, :], w1_h[:, :]))
    emit("GP", [], lambda g: g.dma_start(b1_sb[:, :], b1_h[:, :]))
    emit("GP", [], lambda g: g.dma_start(w2_sb[:, :], w2_h[:, :]))
    emit("GP", [], lambda g: g.dma_start(bin_sb[:, :], bin_h[:, :]))
    emit("GP", [], lambda g: g.dma_start(b2bf[:, :], b2_h[:, :]))
    emit("GP", [], lambda g: g.dma_start(bout_sb[:, :], bout_h[:, :]))
    emit("GP", [], lambda g: g.dma_start(ident_bf[:, :], ident_h[:, :]))
    emit("GP", [], lambda g: g.dma_start(dmask_bf[:, :], dmask_h[:, :]))
    emit("GP", [], lambda g: g.dma_start(wqk_bf[:, :, :, :], wqk_h[:, :, :, :]))
    gp_ld = emit("GP", [], lambda g: g.dma_start(wv_bf[:, :, :, :], wv_h[:, :, :, :]))
    emit("G", [], lambda g: g.memset(ones1[:, :], 1.0))
    emit("G", [], lambda g: g.memset(
        AP(Vst, 64, [[16640, 128], [1040, 16], [65, 16], [1, 1]]), 1.0))
    emit("G", [], lambda g: g.memset(eps_t[:, :], EPS))
    emit("G", [], lambda g: g.memset(ln8_t[:, :], LN8))
    emit("G", [], lambda g: g.memset(Pbuf[:, :, :, :], 0.0))
    emit("G", [], lambda g: g.memset(obc[:, :], 1.0))
    emit("G", [], lambda g: g.memset(bones[:, :], 0.0))
    emit("G", [], lambda g: g.memset(bones[0:64, 0:1], 1.0))
    g_pro = emit("G", [], lambda g: g.memset(bones[64:128, 1:2], 1.0))

    def x_load(ch):
        off = ch * CH * D
        slot = ch % 2

        def f(sy, off=off, slot=slot):
            return sy.dma_start(x_sb[:, slot, :, :],
                                AP(xin, off, [[1024, 128], [131072, 4], [1, 1024]]))
        return f
    xld = {0: emit("SX0", [], x_load(0)), 1: emit("SX1", [], x_load(1))}

    t_be = emit("T", [("GP", gp_ld)], lambda t: (
        t.matmul(ps[0:1, 0:512], b2bf[:, :], wout_bf[:, 0:512], start=True, stop=True),
        t.matmul(ps[0:1, 512:1024], b2bf[:, :], wout_bf[:, 512:1024],
                 start=True, stop=True))[-1])
    emit("V", [("T", t_be)],
         lambda v: v.tensor_add(bout2[:, :], bout_sb[:, :], ps[0:1, 0:1024]))
    emit("V", [], lambda v: v.tensor_copy(boute_bf[:, :], bout2[:, :]))
    emit("V", [("GP", gp_ld)],
         lambda v: v.tensor_scalar(b1t[:, :], w1_sb[:, :], bin_sb[:, 0:1], None, ALU.mult))
    v_pro = emit("V", [], lambda v: v.tensor_add(b1p[:, :], b1t[:, :], b1_sb[:, :]))
    emit("V", [], lambda v: v.memset(ps[0:128, 3072:3584], 0.0))
    v_psz = emit("V", [], lambda v: v.memset(ps[0:128, 0:1024], 0.0))

    M = dict(last_tp_evac=0, outproj_T=0, last_hS=0,
             last_F7_V=v_psz, last_exp_S=0, xln_T=0,
             qn_last_V=0, f1_T=0, bcmm_T0=0, rr_sub_V=0)
    mvS = [0, 0]
    xln_slot_T = [0, 0]
    b_slot_AV_T = [0, 0]
    bc_slot_V = [0, 0]
    hb_slot_V = [0, 0]
    stageB_SY = [0]
    wq_slot_T = [0, 0]
    wv_slot_T = [0, 0]
    attnU_mult_V = [0, 0, 0, 0]
    attn_bank_evac_V = [0, 0]
    evac_qk = {}
    v_evac = {}
    b45 = [v_psz, v_psz]
    mult_cnt = {}
    a_done = {}
    b_doneV = {}
    b_doneT = {}
    bc_of = {}
    gnm_of = {}

    def emit_A_chain(ch, ti):
        slot = ch % 2
        ms = ti % 2
        xv = x_sb[:, slot, ti, :]
        fw = [("SX%d" % (ch % 2), xld[ch])] if ti == 0 else []
        emit("V", fw + [("S", mvS[ms])],
             lambda v, xv=xv, ms=ms: v.bn_stats(bn1[:, ms, 0:6], xv[:, 0:512]))
        emit("V", [],
             lambda v, xv=xv, ms=ms: v.bn_stats(bn1[:, ms, 6:12], xv[:, 512:1024]))
        vc = emit("V", [], lambda v, ms=ms: v.bn_aggr(
            mv[:, ms, :], bn1[:, ms, :].rearrange("p (g s) -> p g s", s=6)))
        emit("S", [("V", vc), ("G", g_pro)], lambda s, ms=ms: s.activation(
            lnv[:, ms, :], mv[:, ms, 1:2], AF.Ln, bias=eps_t[:, 0:1], scale=1.0))
        mvS[ms] = emit("S", [], lambda s, ms=ms: s.activation(
            rstd[:, ms, :], lnv[:, ms, :], AF.Exp, bias=0.0, scale=-0.5))
        emit("V", [("S", mvS[ms])], lambda v, ms=ms: v.tensor_scalar(
            negmr[:, ms, :], mv[:, ms, 0:1], rstd[:, ms, 0:1], -1.0,
            ALU.mult, ALU.mult))
        vx = emit("V", [("T", xln_slot_T[ms])],
                  lambda v, xv=xv, ms=ms: v.tensor_scalar(
                      xln[:, ms, :], xv, rstd[:, ms, 0:1], negmr[:, ms, 0:1],
                      ALU.mult, ALU.add))

        def tp_part(ms=ms, ti=ti, vx=vx):
            def tf(t):
                ins = None
                for c in range(8):
                    ins = t.transpose(ps_tp[:, c * 128:(c + 1) * 128],
                                      xln[:, ms, c * 128:(c + 1) * 128],
                                      ident_bf[:, :])
                return ins
            xln_slot_T[ms] = emit("T", [("V", vx), ("V", M["last_tp_evac"]),
                                        ("GP", gp_ld)], tf)
            M["last_tp_evac"] = emit("V", [("T", xln_slot_T[ms])],
                                     lambda v, ti=ti: v.tensor_copy(
                                         AP(xlnT, ti * 128,
                                            [[4096, 128], [512, 8], [1, 128]]),
                                         ps_tp[:, 0:1024]))
        return tp_part

    def gen_AB(ch):
        gs = ch % NCH_SEQ
        qsl = ch % 2
        if gs == 0:
            gnm_of[ch] = emit("G", [("V", C["V"])],
                              lambda g: g.memset(nmax[:, :, :], 0.0))
        else:
            gnm_of[ch] = gnm_of[ch - 1]
        g_nm = gnm_of[ch]
        tp_pend = []
        for ti in range(NTILE):
            tp_pend.append(emit_A_chain(ch, ti))
            if ti >= 1:
                tp_pend.pop(0)()
            yield
        while tp_pend:
            tp_pend.pop(0)()
            yield
        a_done = M["last_tp_evac"]

        wq_gd = {}
        wq_gd[0] = emit("WQ0", [("T", wq_slot_T[0]), ("GP", gp_ld)],
                        lambda g: g.dma_start(
                            wq_sb[:, 0, :, :],
                            AP(wqk_bf, 0, [[128, 128], [16384, 8], [1, 128]])))
        qn_prev = 0
        pend_norm = []
        sqT = [0, 0, 0]

        def flush_norm():
            nonlocal qn_prev
            og, gsq = pend_norm.pop(0)
            hp = og % 8
            tpj = 32 * (hp % 4)
            sq4 = og % 3

            def nf(t, tpj=tpj, sq4=sq4):
                return t.matmul(ps[tpj:tpj + 2, 3072:3584],
                                bones[:, :], sq_sb[:, sq4, :], start=True, stop=True,
                                tile_position=(0, tpj))
            nrm_T = emit("T", [("G", gsq), ("V", v_psz), ("V", qn_prev),
                               ("S", M["last_hS"])], nf)
            sqT[sq4] = nrm_T
            if hp % 4 == 3:
                emit("V", [("T", nrm_T)],
                     lambda v: v.reduce_max(ntmp[:, :],
                                            ps[0:128, 3072:3584], AX.X))
                cidx = og // 4

                def mxf(v, cidx=cidx, gs=gs):
                    return v.tensor_max(nmax[:, gs % 2, cidx:cidx + 1],
                                        nmax[:, (gs + 1) % 2, cidx:cidx + 1],
                                        ntmp[:, :])
                qn_prev = emit("V", [("G", g_nm)], mxf)

        for og in range(16):
            ob = og % 2
            if og < 15:
                off = (og + 1) * 8 * 128 * 128
                ob2 = (og + 1) % 2

                def wf(g, off=off, ob2=ob2):
                    return g.dma_start(
                        wq_sb[:, ob2, :, :],
                        AP(wqk_bf, off, [[128, 128], [16384, 8], [1, 128]]))
                wq_gd[og + 1] = emit("WQ%d" % ob2, [("T", wq_slot_T[ob2])], wf)

            def qf(t, ob=ob):
                ins = None
                for c in range(8):
                    ins = t.matmul(ps[:, (4 + ob) * 512:(5 + ob) * 512],
                                   wq_sb[:, ob, c, :], xlnT[:, c, :],
                                   start=(c == 0), stop=(c == 7))
                return ins
            wq_slot_T[ob] = emit("T", [("WQ%d" % ob, wq_gd[og]), ("V", a_done),
                                       ("V", b45[ob])], qf)
            hp = og % 8
            if og < 8:
                def ef(v, hp=hp, ob=ob, qsl=qsl):
                    return v.tensor_copy(QT[:, qsl, hp, :],
                                         ps[:, (4 + ob) * 512:(5 + ob) * 512])
                src = QT[:, qsl, hp, :]
            else:
                def ef(v, hp=hp, ob=ob, gs=gs):
                    return v.tensor_copy(KT[:, hp, gs * 512:(gs + 1) * 512],
                                         ps[:, (4 + ob) * 512:(5 + ob) * 512])
                src = KT[:, hp, gs * 512:(gs + 1) * 512]
            evac_qk[og] = emit("V", [("T", wq_slot_T[ob])], ef)
            b45[ob] = evac_qk[og]
            gsq = emit("G", [("V", evac_qk[og]), ("T", sqT[og % 3])],
                       lambda g, src=src, sq4=og % 3: g.tensor_mul(
                           sq_sb[:, sq4, :], src, src))
            pend_norm.append((og, gsq))
            if len(pend_norm) > 2:
                flush_norm()
            yield
        while pend_norm:
            flush_norm()
        M["qn_last_V"] = qn_prev

        for vg in range(2):
            for tp in range(2):
                for hf in range(2):
                    off = (vg * 8 + hf * 4) * 128 * 512

                    def vf(g, off=off, hf=hf):
                        return g.dma_start(
                            wv_sb[:, hf, :, :],
                            AP(wv_bf, off, [[512, 128], [65536, 4], [1, 512]]))
                    wv_gd = emit("WV%d" % hf, [("T", wv_slot_T[hf]), ("GP", gp_ld)], vf)
                    for cc in range(4):
                        for ti in (2 * tp, 2 * tp + 1):
                            def vmf(t, hf=hf, cc=cc, ti=ti):
                                return t.matmul(
                                    ps[:, (4 + ti % 2) * 512:(5 + ti % 2) * 512],
                                    xlnT[:, hf * 4 + cc, ti * 128:(ti + 1) * 128],
                                    wv_sb[:, hf, cc, :],
                                    start=(hf == 0 and cc == 0),
                                    stop=(hf == 1 and cc == 3))
                            w = [("WV%d" % hf, wv_gd)]
                            if hf == 0 and cc == 0:
                                w += [("V", b45[0]), ("V", b45[1])]
                            wv_slot_T[hf] = emit("T", w, vmf)
                for ti in (2 * tp, 2 * tp + 1):
                    def vef(v, vg=vg, ti=ti, gs=gs):
                        j = gs * 4 + ti
                        return v.tensor_copy(
                            AP(Vst, (vg * 8) * 1040 + j * 65,
                               [[16640, 128], [1040, 8], [1, 64]]),
                            ps[:, (4 + ti % 2) * 512:(5 + ti % 2) * 512])
                    v_evac[(vg, tp, ti)] = emit("V", [("T", wv_slot_T[1])], vef)
                    b45[ti % 2] = v_evac[(vg, tp, ti)]
                yield
        b_doneV[ch] = v_evac[(1, 1, 3)]
        b_doneT[ch] = wv_slot_T[1]

        vp = emit("V", [("V", M["qn_last_V"])],
                  lambda v, gs=gs: v.tensor_mul(
                      nprod[:, :], nmax[:, gs % 2, 0:2], nmax[:, gs % 2, 2:4]))
        emit("S", [("V", vp), ("G", g_pro)], lambda s: s.activation(
            lnp[:, :], nprod[:, :], AF.Ln, bias=eps_t[:, 0:1], scale=1.0))
        sp = emit("S", [("G", g_pro)], lambda s: s.activation(
            negc[:, :], lnp[:, :], AF.Exp, bias=ln8_t[0:128, 0:1], scale=0.5))
        vn = emit("V", [("S", sp)], lambda v: v.tensor_scalar(
            negc2[:, :], negc[:, :], -1.0, None, ALU.mult))
        emit("SNC", [("V", vn), ("SCB", C["SCB"])],
             lambda sy, qsl=qsl: sy.dma_start(scr98[qsl, :, :], negc2[0:98, :]))
        bc_of[ch] = emit("SCB", [("SNC", C["SNC"]), ("S", M["last_exp_S"])],
                         lambda sy, qsl=qsl: sy.dma_start(
                             cbc[:, qsl, :],
                             AP(scr98, qsl * 196, [[0, 128], [1, 196]])))
        yield

    def emit_C(ch, gens):
        gs = ch % NCH_SEQ
        qsl = ch % 2

        def pull():
            while gens:
                try:
                    next(gens[0])
                    return
                except StopIteration:
                    gens.pop(0)

        def drain_first():
            if gens:
                for _ in gens[0]:
                    pass
                gens.pop(0)
        pending_post = []
        for h in range(H):
            hp, ab = h // 2, h % 2
            pb = 64 * ab
            abk = 2 + (h % 2)
            nb2 = 2 * (gs + 1)
            for b2 in range(nb2):
                if b2 == 1 and pending_post:
                    pending_post.pop(0)()
                diag = (b2 >= 2 * gs)
                bs2 = b2 % 2

                def sf(t, b2=b2, hp=hp, pb=pb, diag=diag, gs=gs, qsl=qsl):
                    ins = None
                    for jj2 in range(2):
                        j = 2 * b2 + jj2
                        jg = j - 4 * gs
                        q0 = jg * 128
                        ins = t.matmul(ps[:, jj2 * 512:(jj2 + 1) * 512],
                                       KT[pb:pb + 64, hp, j * 128:(j + 1) * 128],
                                       QT[pb:pb + 64, qsl, hp, :],
                                       start=True, stop=diag is False)
                        if diag:
                            ins = t.matmul(
                                ps[:, jj2 * 512:jj2 * 512 + q0 + 128],
                                ident_bf[:, :],
                                dmask_bf[:, (3 - jg) * 128:(4 - jg) * 128 + q0],
                                start=False, stop=True)
                    return ins
                sw = [("S", M["last_exp_S"]), ("V", b_doneV[ch]), ("G", g_pro)]
                sc_T = emit("T", sw, sf)

                ui = 64 * ((h // 2) % 4) + 2 * (h % 2) + (h // 8)

                def xf(s, bs2=bs2, ui=ui, qsl=qsl):
                    return s.activation(
                        Pbuf[:, bs2, :, :].rearrange("p a b -> p (a b)"),
                        ps[:, 0:1024], AF.Exp, bias=cbc[:, qsl, ui:ui + 1],
                        scale=0.125)
                M["last_exp_S"] = emit("S", [("T", sc_T), ("SCB", bc_of[ch]),
                                             ("T", b_slot_AV_T[bs2])], xf)
                pull()

                def af(t, b2=b2, bs2=bs2, h=h, abk=abk, nb2=nb2):
                    ins = None
                    for jj2 in range(2):
                        j = 2 * b2 + jj2
                        ins = t.matmul(ps[0:65, abk * 512:(abk + 1) * 512],
                                       Vst[:, h, j, 0:65],
                                       Pbuf[:, bs2, jj2, :],
                                       start=(b2 == 0 and jj2 == 0),
                                       stop=(b2 == nb2 - 1 and jj2 == 1))
                    return ins
                avw = [("S", M["last_exp_S"])]
                if b2 == 0:
                    avw.append(("V", attn_bank_evac_V[h % 2]))
                b_slot_AV_T[bs2] = emit("T", avw, af)
            av_T = b_slot_AV_T[(nb2 - 1) % 2]
            us = h % 4

            def uf(v, us=us, abk=abk):
                return v.tensor_copy(attnU[:, us, :],
                                     ps[0:65, abk * 512:(abk + 1) * 512])
            ue = emit("V", [("T", av_T), ("V", attnU_mult_V[us])], uf)
            if h == 1 and gens and getattr(gens[0], "is_de", False):
                drain_first()
            emit("S", [("V", ue)],
                 lambda s, us=us: s.activation(
                     rr65[64:65, 0, :], attnU[64:65, us, :], AF.Ln,
                     bias=0.0, scale=1.0))
            sr = emit("S", [("V", M["rr_sub_V"])], lambda s: s.activation(
                rr65[64:65, 1, :], rr65[64:65, 0, :], AF.Exp,
                bias=0.0, scale=-1.0))
            emit("V", [("T", M["bcmm_T0"]), ("S", sr)],
                 lambda v: v.tensor_copy(rrb[64:65, 0, :], rr65[64:65, 1, :]))
            s2 = emit("V", [], lambda v: v.tensor_sub(
                rrb[64:65, 1, :], rr65[64:65, 1, :], rrb[64:65, 0, :]))
            M["rr_sub_V"] = s2

            def post(h=h, hp=hp, ab=ab, us=us, abk=abk, s2=s2, ch=ch):
                def bcm(t, abk=abk):
                    t.matmul(ps[0:64, abk * 512:(abk + 1) * 512],
                             obc[64:65, 0:64], rrb[64:65, 0, :],
                             start=True, stop=False)
                    return t.matmul(ps[0:64, abk * 512:(abk + 1) * 512],
                                    obc[64:65, 0:64], rrb[64:65, 1, :],
                                    start=False, stop=True)
                bcmm = emit("T", [("V", s2)], bcm)
                M["bcmm_T0"] = bcmm
                if ab == 0:
                    dst = attnT[0:64, hp, :]
                else:
                    dst = attnB[:, :]

                def mf(v, dst=dst, us=us, abk=abk):
                    return v.tensor_mul(dst, attnU[0:64, us, :],
                                        ps[0:64, abk * 512:(abk + 1) * 512])
                mw = [("T", bcmm), ("T", M["outproj_T"])]
                if ab == 1:
                    mw.append(("SSB", stageB_SY[0]))
                mult_cnt[(ch, h)] = emit("V", mw, mf)
                attnU_mult_V[us] = mult_cnt[(ch, h)]
                attn_bank_evac_V[h % 2] = mult_cnt[(ch, h)]
                if ab == 1:
                    def sbf(sy, hp=hp):
                        return sy.dma_start(attnT[64:128, hp, :], attnB[:, :])
                    stageB_SY[0] = emit("SSB", [("V", mult_cnt[(ch, h)])], sbf)
            pending_post.append(post)
        while pending_post:
            pending_post.pop(0)()

    def _gen_DE(ch):
        slot = ch % 2
        attn_done_V = mult_cnt[(ch, 14)]
        attn_done_SY = stageB_SY[0]
        d3_V = 0
        tp_pend = []
        for ti in range(NTILE):
            def of(t, ti=ti):
                ins = None
                for hp in range(8):
                    for mg in range(2):
                        ins = t.matmul(ps[:, (4 + mg) * 512:(5 + mg) * 512],
                                       attnT[:, hp, ti * 128:(ti + 1) * 128],
                                       woT[:, hp, mg * 512:(mg + 1) * 512],
                                       start=(hp == 0), stop=(hp == 7))
                return ins
            ow = [("V", attn_done_V), ("SSB", attn_done_SY), ("V", d3_V),
                  ("V", b45[0]), ("V", b45[1])]
            M["outproj_T"] = emit("T", ow, of)
            op_T = M["outproj_T"]

            def df(v, ti=ti, slot=slot):
                return v.tensor_add(x_sb[:, slot, ti, :], x_sb[:, slot, ti, :],
                                    ps[:, 2048:3072])
            d3_V = emit("V", [("T", op_T)], df)
            b45[0] = d3_V
            b45[1] = d3_V
            tp_pend.append(emit_A_chain(ch, ti))
            if ti >= 1:
                tp_pend.pop(0)()
            yield

        while tp_pend:
            tp_pend.pop(0)()
            yield

        def f1(t):
            ins = None
            for c in range(8):
                ins = t.matmul(ps[0:64, 3072:3584], win_bf[:, c, :], xlnT[:, c, :],
                               start=(c == 0), stop=(c == 7))
            return ins
        M["f1_T"] = emit("T", [("V", M["last_tp_evac"]), ("S", M["last_hS"]),
                               ("V", M["qn_last_V"])], f1)
        yield

    def _gen_F(ch):
        slot = ch % 2
        f1_T = M["f1_T"]
        for i in range(16):
            hs = i % 2

            def hf2(s, i=i, hs=hs):
                return s.activation(h_sb[:, hs, :], ps[0:64, 3072:3584],
                                    AF.Relu, bias=b1p[:, i:i + 1],
                                    scale=w1_sb[:, i:i + 1])
            sH = emit("S", [("T", f1_T), ("V", hb_slot_V[hs])], hf2)
            M["last_hS"] = sH
            if i == 0:
                mac_V = emit("V", [("S", sH)],
                             lambda v: v.tensor_scalar(g_sb[:, 0, :], h_sb[:, 0, :],
                                                       w2_sb[:, 0:1], None, ALU.mult))
            else:
                def mi(v, i=i, hs=hs):
                    return v.scalar_tensor_tensor(g_sb[:, i % 2, :], h_sb[:, hs, :],
                                                  w2_sb[:, i:i + 1],
                                                  g_sb[:, (i + 1) % 2, :],
                                                  ALU.mult, ALU.add)
                mac_V = emit("V", [("S", sH)], mi)
            hb_slot_V[hs] = mac_V
            if i % 4 == 3:
                yield
        gb_V = emit("V", [], lambda v: v.tensor_copy(gbf[:, :], g_sb[:, 1, :]))
        yield
        f7_V = 0
        for ti in range(NTILE):
            def f6(t, ti=ti):
                t.matmul(ps[:, 2048:2560], gbf[:, ti * 128:(ti + 1) * 128],
                         wout_bf[:, 0:512], start=True, stop=False)
                t.matmul(ps[:, 2560:3072], gbf[:, ti * 128:(ti + 1) * 128],
                         wout_bf[:, 512:1024], start=True, stop=False)
                t.matmul(ps[:, 2048:2560], ones1[:, :], boute_bf[:, 0:512],
                         start=False, stop=True)
                return t.matmul(ps[:, 2560:3072], ones1[:, :], boute_bf[:, 512:1024],
                                start=False, stop=True)
            f6_T = emit("T", [("V", gb_V), ("V", f7_V if ti > 0 else b45[0]),
                              ("V", b45[1]), ("G", g_pro)], f6)

            def f7(v, ti=ti, slot=slot):
                return v.tensor_add(x_sb[:, slot, ti, :], x_sb[:, slot, ti, :],
                                    ps[:, 2048:3072])
            f7_V = emit("V", [("T", f6_T)], f7)
            b45[0] = f7_V
            b45[1] = f7_V
            yield
        M["last_F7_V"] = f7_V

        off = ch * CH * D

        def f8(sy, off=off, slot=slot):
            return sy.dma_start(AP(outx, off, [[1024, 128], [131072, 4], [1, 1024]]),
                                x_sb[:, slot, :, :])
        emit("SOUT", [("V", f7_V)], f8)
        if ch + 2 < NCH:
            xld[ch + 2] = emit("SX%d" % (ch % 2), [("SOUT", C["SOUT"])],
                               x_load(ch + 2))
        yield

    class _DeGen:
        is_de = True

        def __init__(self, g):
            self.g = g

        def __iter__(self):
            return self.g

        def __next__(self):
            return next(self.g)

    def gen_DE(ch):
        return _DeGen(_gen_DE(ch))

    nch = NCH if LIMIT_CH is None else LIMIT_CH
    for _ in gen_AB(0):
        pass
    pend = []
    for ch in range(nch):
        boundary = (ch + 1 < nch) and ((ch + 1) % NCH_SEQ == 0)
        gens = list(pend)
        if (ch + 1 < nch) and not boundary:
            gens.append(gen_AB(ch + 1))
        emit_C(ch, gens)
        for g in gens:
            for _ in g:
                pass
        if boundary:
            for _ in gen_DE(ch):
                pass
            gf = _gen_F(ch)
            ga = gen_AB(ch + 1)
            for _ in range(5):
                try:
                    next(gf)
                except StopIteration:
                    pass
                try:
                    next(ga)
                except StopIteration:
                    pass
            for _ in gf:
                pass
            for _ in ga:
                pass
            pend = []
        else:
            pend = [gen_DE(ch), _gen_F(ch)]
    for g in pend:
        for _ in g:
            pass

    final_sy = C["SOUT"]

    def fin(eng):
        eng.wait_ge(SEM["SOUT"], final_sy)
        eng.nop()
    plan["sync"].append(fin)

    with nc.Block() as block:
        @block.tensor
        def _(t):
            for f in plan["tensor"]:
                f(t)

        @block.vector
        def _(v):
            for f in plan["vector"]:
                f(v)

        @block.scalar
        def _(s):
            for f in plan["scalar"]:
                f(s)

        @block.gpsimd
        def _(g):
            for f in plan["gpsimd"]:
                f(g)

        @block.sync
        def _(sy):
            for f in plan["sync"]:
                f(sy)

    return nc


def _host_layouts(inputs):
    w_qkv = np.asarray(inputs["L0_attn_w_qkv"], dtype=np.float32)
    w_o = np.asarray(inputs["L0_attn_w_o"], dtype=np.float32)
    Win = np.asarray(inputs["L0_ffn_Win"], dtype=np.float32)
    bin_ = np.asarray(inputs["L0_ffn_bin_"], dtype=np.float32)
    W1 = np.asarray(inputs["L0_ffn_W1"], dtype=np.float32)
    b1 = np.asarray(inputs["L0_ffn_b1"], dtype=np.float32)
    W2 = np.asarray(inputs["L0_ffn_W2"], dtype=np.float32)
    b2 = np.asarray(inputs["L0_ffn_b2"], dtype=np.float32)
    Wout = np.asarray(inputs["L0_ffn_Wout"], dtype=np.float32)
    bout = np.asarray(inputs["L0_ffn_bout"], dtype=np.float32)

    wq = w_qkv[0].reshape(8, 128, 1024)
    wk = w_qkv[1].reshape(8, 128, 1024)
    wqk = np.concatenate([wq, wk], axis=0)
    wqk = wqk.transpose(0, 2, 1).reshape(16, 8, 128, 128)
    wvm = w_qkv[2].reshape(1024, 1024).T
    wv = wvm.reshape(8, 128, 2, 512).transpose(2, 0, 1, 3)
    wo = w_o.T.reshape(8, 128, 1024)
    win = Win.reshape(8, 128, 64)

    ident = np.eye(128, dtype=np.float32)
    p = np.arange(128)[:, None]
    w = np.arange(896)[None, :]
    dmask = np.where(p > w - 384, np.float32(-1e30), np.float32(0.0))

    return {
        "wqk": np.ascontiguousarray(wqk),
        "wv": np.ascontiguousarray(wv),
        "wo": np.ascontiguousarray(wo),
        "win": np.ascontiguousarray(win),
        "w1": np.ascontiguousarray(W1),
        "b1": np.ascontiguousarray(b1),
        "w2": np.ascontiguousarray(W2),
        "binv": np.ascontiguousarray(bin_.reshape(64, 1)),
        "b2v": np.ascontiguousarray(b2.reshape(64, 1)),
        "boutv": np.ascontiguousarray(bout.reshape(1, 1024)),
        "woutm": np.ascontiguousarray(Wout),
        "ident": ident,
        "dmask": np.ascontiguousarray(dmask),
    }


def kernel(**inputs):
    x = np.asarray(inputs["L0_attn_x"], dtype=np.float32)
    shared = _host_layouts(inputs)
    nc = _plan_program()
    in_maps = []
    for c in range(NCORES):
        m = dict(shared)
        m["xin"] = np.ascontiguousarray(x[c * BPC:(c + 1) * BPC].reshape(BPC * S, D))
        in_maps.append(m)
    res = run_bass_kernel_spmd(nc, in_maps, core_ids=list(range(NCORES)))
    out = np.concatenate(
        [np.asarray(r["outx"]).reshape(BPC, S, D) for r in res.results], axis=0)
    return out
